# version 02
# baseline (speedup 1.0000x reference)
import numpy as np
import ml_dtypes
from contextlib import ExitStack

import concourse.bass as bass
import concourse.mybir as mybir
from concourse.bass_utils import run_bass_kernel_spmd

F32 = mybir.dt.float32
BF16 = mybir.dt.bfloat16
F16 = mybir.dt.float16
I32 = mybir.dt.int32
AF = mybir.ActivationFunctionType
ALU = mybir.AluOpType
BF16NP = ml_dtypes.bfloat16

NCORE = 8
D = 2048
NCH = 16
T = 1024
SL = 256
HID0 = 5632
NH0 = 44
I1 = 5632
GU1 = 11264
EPS_LN = 1e-5
EPS_RMS = 1e-6
PI = float(np.pi)



class Sem:

    def __init__(self, handle):
        self.h = handle
        self.n = 0

    def inc(self, instr, k=1):
        instr.then_inc(self.h, k)
        self.n += k
        return self.n


def halves(n=1024, step=512):
    return [(i, min(i + step, n)) for i in range(0, n, step)]



def _rope_tables(core):
    s = np.arange(SL, dtype=np.float64) + SL * core
    inv_freq = 1.0 / (10000.0 ** (np.arange(0, 128, 2, dtype=np.float64) / 128.0))
    freqs = np.outer(s, inv_freq)
    emb = np.concatenate([freqs, freqs], 1)
    cos = np.cos(emb).astype(np.float32).T
    sin = np.sin(emb).astype(np.float32).T
    cos = np.tile(cos, (1, 4))
    sin = np.tile(sin, (1, 4))
    return np.ascontiguousarray(cos), np.ascontiguousarray(sin)


def _staircase32():
    t = np.zeros((128, 33 * 31 + 32), dtype=np.float32)
    t[:, ::33] = 1.0
    return t


def host_prep(inputs):
    sh = {}
    C = np.ascontiguousarray

    sh["l0_WqT"] = C(inputs["L0_attn_Wq"].T)
    sh["l0_WkT"] = C(inputs["L0_attn_Wk"].T)
    sh["l0_WvT"] = C(inputs["L0_attn_Wv"].T)
    sh["l0_WoT"] = C(inputs["L0_attn_Wo"].T)
    sh["l0_bq"] = C(inputs["L0_attn_bq"].reshape(16, 128).T)
    sh["l0_bk"] = C(inputs["L0_attn_bk"].reshape(8, 128).T)
    sh["l0_bv"] = C(inputs["L0_attn_bv"].reshape(1, 1024))
    sh["l0_bo"] = C(inputs["L0_attn_bo"].reshape(16, 128).T)
    R = inputs["L0_attn_rand_matrix"]
    sh["l0_RTT"] = C(np.concatenate([R.T, R.T], 1))

    sh["l0_WgT"] = C(inputs["L0_ffn_wq_gate"].T)
    sh["l0_WuT"] = C(inputs["L0_ffn_wq_up"].T)
    sh["l0_WdT"] = C(inputs["L0_ffn_wq_down"].T)
    sh["l0_sg"] = C(inputs["L0_ffn_s_gate"].reshape(NH0, 128).T)
    sh["l0_su"] = C(inputs["L0_ffn_s_up"].reshape(NH0, 128).T)
    sh["l0_sd"] = C(inputs["L0_ffn_s_down"].reshape(16, 128).T)

    wqkv = inputs["L1_attn_weight_qkv"]
    q_idx = np.concatenate([
        np.arange(128) + (h // 4) * 768 + (h % 4) * 128 for h in range(16)])
    k_idx = np.concatenate([np.arange(128) + g * 768 + 512 for g in range(4)])
    v_idx = np.concatenate([np.arange(128) + g * 768 + 640 for g in range(4)])
    sh["l1_WqT"] = C(wqkv[q_idx].T)
    sh["l1_WkT"] = C(wqkv[k_idx].T)
    sh["l1_WvT"] = C(wqkv[v_idx].T)
    sh["l1_WpT"] = C(inputs["L1_attn_weight_proj"].T)

    gup = inputs["L1_ffn_gate_up_packed"]
    gup_T = C(gup.T)
    gsc = inputs["L1_ffn_gate_up_scales"]
    dp = inputs["L1_ffn_down_packed"]
    dp_T = C(dp.T)
    dsc = inputs["L1_ffn_down_scales"]

    sh["ident"] = np.eye(128, dtype=np.float32)
    prot = np.zeros((128, 128), dtype=np.float32)
    for mm_ in range(64):
        prot[mm_ + 64, mm_] = -1.0
        prot[mm_, mm_ + 64] = 1.0
    sh["l0_prot"] = prot
    sh["t32"] = _staircase32()


    per_core = []
    x = inputs["L0_attn_hidden_states"]
    for c in range(NCORE):
        pc = {}
        xs = x[:, SL * c:SL * (c + 1), :]
        pc["x"] = C(xs.transpose(2, 0, 1).reshape(D, T))
        cos, sin = _rope_tables(c)
        pc["rope_cos"] = cos
        pc["rope_sin"] = sin
        pb, oq = c // 4, c % 4
        pc["dq_gup"] = C(gup_T[128 * pb:128 * (pb + 1),
                               2816 * oq:2816 * (oq + 1)])
        pc["dq_gup_s"] = C(gsc[8 * pb + np.arange(128) // 16,
                               2816 * oq:2816 * (oq + 1)])
        pc["dq_dn"] = C(dp_T[88 * c:88 * (c + 1)])
        pc["dq_dn_s"] = C(dsc[(88 * c + np.arange(88)) // 16])
        per_core.append(pc)
    return sh, per_core



def build_dequant():
    nc = bass.Bass("TRN2", target_bir_lowering=False, debug=False)
    gup = nc.dram_tensor("dq_gup", [128, 2816], I32, kind="ExternalInput").ap()
    gup_s = nc.dram_tensor("dq_gup_s", [128, 2816], F16, kind="ExternalInput").ap()
    dn = nc.dram_tensor("dq_dn", [88, 2048], I32, kind="ExternalInput").ap()
    dn_s = nc.dram_tensor("dq_dn_s", [88, 2048], F16, kind="ExternalInput").ap()
    l0g = nc.dram_tensor("dq_l0g", [256, HID0], I32, kind="ExternalInput").ap()
    l0u = nc.dram_tensor("dq_l0u", [256, HID0], I32, kind="ExternalInput").ap()
    l0d = nc.dram_tensor("dq_l0d", [704, D], I32, kind="ExternalInput").ap()
    o_l0g = nc.dram_tensor("o_l0g", [256, HID0], BF16, kind="ExternalOutput").ap()
    o_l0u = nc.dram_tensor("o_l0u", [256, HID0], BF16, kind="ExternalOutput").ap()
    o_l0d = nc.dram_tensor("o_l0d", [704, D], BF16, kind="ExternalOutput").ap()
    o_gup = nc.dram_tensor("o_gup", [8, 128, 2816], BF16, kind="ExternalOutput").ap()
    o_dn = nc.dram_tensor("o_dn", [8, 88, 2048], BF16, kind="ExternalOutput").ap()

    l0_tiles = []
    for tix in range(2):
        l0_tiles.append((l0g[128 * tix:128 * (tix + 1), :],
                         o_l0g[128 * tix:128 * (tix + 1), :], 128, HID0))
    for tix in range(2):
        l0_tiles.append((l0u[128 * tix:128 * (tix + 1), :],
                         o_l0u[128 * tix:128 * (tix + 1), :], 128, HID0))
    for tix in range(6):
        r0 = 128 * tix
        rr = min(128, 704 - r0)
        l0_tiles.append((l0d[r0:r0 + rr, :], o_l0d[r0:r0 + rr, :], rr, D))

    with ExitStack() as S:
        sb = lambda n, shp, dt=F32: S.enter_context(nc.sbuf_tensor(n, shp, dt))
        sem = lambda n: S.enter_context(nc.semaphore(n))
        pk_g = sb("pk_g", [128, 2816], I32)
        sc_g = sb("sc_g", [128, 2816], F16)
        pk_d = sb("pk_d", [88, 2048], I32)
        sc_d = sb("sc_d", [88, 2048], F16)
        u1 = sb("u1", [128, 2816], I32)
        nib = sb("nib", [128, 2816], I32)
        sbit = sb("sbit", [128, 2816], BF16)
        mb = sb("mb", [128, 2816], BF16)
        r1 = sb("r1", [128, 2816], BF16)
        r2 = sb("r2", [128, 2816], BF16)
        f1 = sb("f1", [128, 2816], BF16)
        ff = sb("ff", [128, 2816], BF16)
        gg = sb("gg", [128, 2816], BF16)
        wout = [sb(f"wout{i}", [128, 2816], BF16) for i in range(2)]
        l0in = [sb(f"l0in{i}", [128, HID0], I32) for i in range(2)]
        l0out = [sb(f"l0out{i}", [128, HID0], BF16) for i in range(2)]
        s_ld = Sem(sem("s_ld"))
        s_l0 = Sem(sem("s_l0"))
        s_dv = Sem(sem("s_dv"))
        s_st = Sem(sem("s_st"))
        block = S.enter_context(nc.Block())

        @block.sync
        def _(sync):
            for k, (iap, oap, rows, cols) in enumerate(l0_tiles):
                if k >= 2:
                    sync.wait_ge(s_dv.h, k - 1)
                s_l0.inc(sync.dma_start(out=l0in[k % 2][:rows, :cols], in_=iap),
                         16)
            for dst, src in [(pk_g[:], gup), (sc_g[:], gup_s),
                             (pk_d[:], dn), (sc_d[:], dn_s)]:
                s_ld.inc(sync.dma_start(out=dst, in_=src), 16)

        n_l0 = len(l0_tiles)
        plane_milestones = []

        @block.vector
        def _(v):
            for k, (iap, oap, rows, cols) in enumerate(l0_tiles):
                bi = k % 2
                v.wait_ge(s_l0.h, 16 * (k + 1))
                if k >= 2:
                    v.wait_ge(s_st.h, 16 * (k - 1))
                ins = v.tensor_scalar(l0out[bi][:rows, :cols],
                                      l0in[bi][:rows, :cols],
                                      -128, None, ALU.add)
                s_dv.inc(ins)
            v.wait_ge(s_ld.h, 64)
            for part in range(2):
                if part == 0:
                    pk, sc, P, W = pk_g, sc_g, 128, 2816
                else:
                    pk, sc, P, W = pk_d, sc_d, 88, 2048
                for j in range(8):
                    bi = (part * 8 + j) % 2
                    k = n_l0 + part * 8 + j
                    if k >= 2:
                        v.wait_ge(s_st.h, 16 * (k - 1))
                    w = wout[bi]
                    v.tensor_scalar(u1[:P, :W], pk[:, :], 4 * j, None,
                                    ALU.logical_shift_right)
                    v.tensor_scalar(nib[:P, :W], u1[:P, :W], 15, None,
                                    ALU.bitwise_and)
                    v.tensor_scalar(sbit[:P, :W], nib[:P, :W], 8, None,
                                    ALU.is_ge)
                    v.scalar_tensor_tensor(mb[:P, :W], sbit[:P, :W], -8.0,
                                           nib[:P, :W], ALU.mult, ALU.add)
                    v.tensor_scalar(r1[:P, :W], mb[:P, :W], -4.0, 0.0,
                                    ALU.add, ALU.max)
                    v.tensor_scalar(r2[:P, :W], mb[:P, :W], -6.0, 0.0,
                                    ALU.add, ALU.max)
                    v.tensor_add(f1[:P, :W], mb[:P, :W], r1[:P, :W])
                    v.scalar_tensor_tensor(ff[:P, :W], f1[:P, :W], 0.5,
                                           r2[:P, :W], ALU.mult, ALU.add)
                    v.tensor_scalar(gg[:P, :W], sbit[:P, :W], -2.0, 1.0,
                                    ALU.mult, ALU.add)
                    v.tensor_mul(w[:P, :W], ff[:P, :W], gg[:P, :W])
                    ins = v.tensor_mul(w[:P, :W], w[:P, :W], sc[:, :])
                    plane_milestones.append(s_dv.inc(ins))

        @block.gpsimd
        def _(g):
            for k, (iap, oap, rows, cols) in enumerate(l0_tiles):
                g.wait_ge(s_dv.h, k + 1)
                s_st.inc(g.dma_start(out=oap, in_=l0out[k % 2][:rows, :cols]),
                         16)
            for kk in range(16):
                part, j = divmod(kk, 8)
                bi = kk % 2
                g.wait_ge(s_dv.h, plane_milestones[kk])
                if part == 0:
                    s_st.inc(g.dma_start(out=o_gup[j],
                                         in_=wout[bi][:128, :2816]), 16)
                else:
                    s_st.inc(g.dma_start(out=o_dn[j],
                                         in_=wout[bi][:88, :2048]), 16)
            g.wait_ge(s_st.h, s_st.n)

    return nc


def host_gather_dequant(results, per_core):
    G_gu = np.zeros((D, GU1), dtype=BF16NP)
    G_dn = np.zeros((I1, D), dtype=BF16NP)
    G_l0g = np.zeros((D, HID0), dtype=BF16NP)
    G_l0u = np.zeros((D, HID0), dtype=BF16NP)
    G_l0d = np.zeros((HID0, D), dtype=BF16NP)
    for c in range(NCORE):
        pb, oq = c // 4, c % 4
        og = results[c]["o_gup"]
        for j in range(8):
            rows = 1024 * pb + 8 * np.arange(128) + j
            G_gu[rows, 2816 * oq:2816 * (oq + 1)] = og[j]
        od = results[c]["o_dn"]
        for j in range(8):
            rows = 704 * c + 8 * np.arange(88) + j
            G_dn[rows] = od[j]
        G_l0g[256 * c:256 * (c + 1)] = results[c]["o_l0g"]
        G_l0u[256 * c:256 * (c + 1)] = results[c]["o_l0u"]
        G_l0d[704 * c:704 * (c + 1)] = results[c]["o_l0d"]
    return {"G_gu": G_gu, "G_dn": G_dn, "G_l0g": G_l0g, "G_l0u": G_l0u,
            "G_l0d": G_l0d}


def dequant_in_maps(sh, per_core):
    maps = []
    for c, pc in enumerate(per_core):
        m = {k: pc[k] for k in ["dq_gup", "dq_gup_s", "dq_dn", "dq_dn_s"]}
        m["dq_l0g"] = np.ascontiguousarray(sh["l0_WgT"][256 * c:256 * (c + 1)])
        m["dq_l0u"] = np.ascontiguousarray(sh["l0_WuT"][256 * c:256 * (c + 1)])
        m["dq_l0d"] = np.ascontiguousarray(sh["l0_WdT"][704 * c:704 * (c + 1)])
        maps.append(m)
    return maps




KERNEL_DEBUG = False


def run_cores(nc, in_maps):
    res = run_bass_kernel_spmd(nc, in_maps, core_ids=list(range(len(in_maps))))
    return res.results



def emit_ffn(nc, S, layer, aps, ngroups=2, nhid=NH0):
    sb = lambda n, shp, dt=F32: S.enter_context(nc.sbuf_tensor(n, shp, dt))
    ps = lambda n, shp: S.enter_context(nc.psum_tensor(n, shp, F32))
    sem = lambda n: Sem(S.enter_context(nc.semaphore(n)))

    rms = (layer == "l1")
    NG = ngroups
    GCH = (nhid // 128) // NG
    assert (nhid // 128) % NG == 0

    xT = sb("xT", [128, NCH * T])
    xnb = S.enter_context(nc.sbuf_tensor("xnb", [128, NCH * T], BF16))
    sq = S.enter_context(nc.sbuf_tensor("sq", [128, 2 * T], BF16))
    m_sb = sb("m_sb", [128, T])
    rstd = sb("rstd", [128, T])
    tmpf = sb("tmpf", [128, T])
    ones_bf = S.enter_context(nc.sbuf_tensor("ones_bf", [128, 128], BF16))
    eps_col = sb("eps_col", [128, 1])
    wg_buf = S.enter_context(nc.sbuf_tensor("wg_buf", [128, 2 * NCH * 128], BF16))
    wu_buf = S.enter_context(nc.sbuf_tensor("wu_buf", [128, 2 * NCH * 128], BF16))
    wd_buf = S.enter_context(nc.sbuf_tensor("wd_buf", [128, 2 * GCH * 128], BF16))
    gsil = S.enter_context(nc.sbuf_tensor("gsil", [128, 2 * T], BF16))
    hbuf = S.enter_context(nc.sbuf_tensor("hbuf", [128, GCH * T], BF16))
    if layer == "l0":
        sg_cols = sb("sg_cols", [128, nhid // 128])
        su_cols = sb("su_cols", [128, nhid // 128])
        sd_cols = sb("sd_cols", [128, NCH])
        sd05 = sb("sd05", [128, NCH])
    if rms:
        eps2_col = sb("eps2_col", [128, 1])
        rstd05 = sb("rstd05", [128, T])
    ybuf = (S.enter_context(nc.sbuf_tensor("ybuf", [128, NCH * T], BF16))
            if rms else None)

    pp = [ps(f"pp{i}", [128, 512]) for i in range(8)]

    s_x = sem("s_x")
    s_wg = sem("s_wg")
    s_wu = sem("s_wu")
    s_pe = sem("s_pe")
    s_ac = sem("s_ac")
    s_dv = sem("s_dv")
    s_out = sem("s_out")

    NHALF = [(0, 512), (512, 1024)]

    M = {}
    M["xb"] = 1
    M["stats"] = 16
    M["var"] = 2
    M["rstd"] = 17
    M["xn"] = 3
    pe_n = M["stats"]
    ac_n = M["rstd"]
    dv_n = M["xn"]
    M["gmm"] = {}
    M["umm"] = {}
    M["silu"] = {}
    M["h"] = {}
    M["dmm"] = {}
    M["res"] = {}
    for g in range(NG):
        for hc in range(GCH):
            ch = GCH * g + hc
            pe_n += 1
            M["gmm"][ch] = pe_n
            pe_n += 1
            M["umm"][ch] = pe_n
            ac_n += 1
            M["silu"][ch] = ac_n
            dv_n += 1
            M["h"][ch] = dv_n
        for m in range(NCH):
            pe_n += 1
            M["dmm"][(g, m)] = pe_n
            dv_n += 1
            M["res"][(g, m)] = dv_n
    if rms:
        M["ysq0"] = ac_n
        ac_n += 16
        M["ystats"] = pe_n + 16
        pe_n += 16
        dv_n += 1
        M["yvar"] = dv_n
        ac_n += 1
        M["yrstd"] = ac_n
        dv_n += 1
        M["final"] = dv_n
    else:
        M["final"] = dv_n

    wg_cnt = {}
    wu_cnt = {}
    n_wg = 0
    n_wu = 0
    for g in range(NG):
        for hc in range(GCH):
            ch = GCH * g + hc
            n_wg += 32
            wg_cnt[("g", ch)] = n_wg
            n_wu += 32
            wu_cnt[ch] = n_wu
        for m in range(NCH):
            n_wg += 32
            wg_cnt[("d", g, m)] = n_wg

    block = S.enter_context(nc.Block())

    @block.sync
    def _(sync):
        s_x.inc(sync.dma_start(
            out=xT[:].rearrange("p (c t) -> p c t", c=NCH),
            in_=aps["x_in"].rearrange("(c p) t -> p c t", p=128)), 16)
        if layer == "l0":
            for dst, src in [(sg_cols, aps["sg"]), (su_cols, aps["su"]),
                             (sd_cols, aps["sd"])]:
                s_x.inc(sync.dma_start(out=dst[:], in_=src), 16)
        for g in range(NG):
            for hc in range(GCH):
                ch = GCH * g + hc
                par = ch % 2
                if ch >= 2:
                    sync.wait_ge(s_pe.h, M["umm"][ch - 2])
                wv = wg_buf[:].rearrange("p (b c o) -> p b c o", b=2, c=NCH)
                src = aps["gT"][:, 128 * ch:128 * (ch + 1)]
                srcv = src.rearrange("(c p) o -> p c o", p=128)
                for half in range(2):
                    s_wg.inc(sync.dma_start(
                        out=wv[:, par, 8 * half:8 * (half + 1), :],
                        in_=srcv[:, 8 * half:8 * (half + 1), :]), 16)
            for m in range(NCH):
                par = m % 2
                if (g, m) != (0, 0) and (g * NCH + m) >= 2:
                    gm = g * NCH + m - 2
                    g2, m2 = divmod(gm, NCH)
                    sync.wait_ge(s_pe.h, M["dmm"][(g2, m2)])
                wv = wd_buf[:].rearrange("p (b k o) -> p b k o", b=2, k=GCH)
                src = aps["dT"][GCH * 128 * g:GCH * 128 * (g + 1),
                                128 * m:128 * (m + 1)]
                srcv = src.rearrange("(k p) o -> p k o", p=128)
                hks = (GCH + 1) // 2
                for k0, k1 in [(0, hks), (hks, GCH)]:
                    s_wg.inc(sync.dma_start(
                        out=wv[:, par, k0:k1, :],
                        in_=srcv[:, k0:k1, :]), 16)
        sync.wait_ge(s_dv.h, M["final"])
        s_out.inc(sync.dma_start(
            out=aps["x_out"].rearrange("(c p) t -> p c t", p=128),
            in_=xT[:].rearrange("p (c t) -> p c t", c=NCH)), 16)
        sync.wait_ge(s_out.h, 16)

    @block.gpsimd
    def _(gp):
        for g in range(NG):
            for hc in range(GCH):
                ch = GCH * g + hc
                par = ch % 2
                if ch >= 2:
                    gp.wait_ge(s_pe.h, M["umm"][ch - 2])
                wv = wu_buf[:].rearrange("p (b c o) -> p b c o", b=2, c=NCH)
                src = aps["uT"][:, 128 * ch:128 * (ch + 1)]
                srcv = src.rearrange("(c p) o -> p c o", p=128)
                for half in range(2):
                    s_wu.inc(gp.dma_start(
                        out=wv[:, par, 8 * half:8 * (half + 1), :],
                        in_=srcv[:, 8 * half:8 * (half + 1), :]), 16)

    @block.scalar
    def _(act):
        act.wait_ge(s_dv.h, M["xb"])
        xnv = xnb[:].rearrange("p (c t) -> p c t", c=NCH)
        sqv = sq[:].rearrange("p (b t) -> p b t", b=2)
        for c in range(NCH):
            if c >= 2:
                act.wait_ge(s_pe.h, c - 1)
            s_ac.inc(act.activation(sqv[:, c % 2, :], xnv[:, c, :], AF.Square))
        act.wait_ge(s_dv.h, M["var"])
        act.activation(tmpf[:], tmpf[:], AF.Ln, bias=eps_col[:])
        s_ac.inc(act.activation(rstd[:], tmpf[:], AF.Exp, scale=-0.5))
        hv = hbuf[:].rearrange("p (k t) -> p k t", k=GCH)
        gv = gsil[:].rearrange("p (b t) -> p b t", b=2)
        for g in range(NG):
            for hc in range(GCH):
                ch = GCH * g + hc
                par = ch % 2
                act.wait_ge(s_pe.h, M["gmm"][ch])
                if ch >= 2:
                    act.wait_ge(s_dv.h, M["h"][ch - 2])
                pg = pp[0 + 4 * (ch % 2):2 + 4 * (ch % 2)]
                for half in range(2):
                    n0, n1 = NHALF[half]
                    if layer == "l0":
                        ins = act.activation(gv[:, par, n0:n1], pg[half][:],
                                             AF.Silu,
                                             scale=sg_cols[:, ch:ch + 1])
                    else:
                        ins = act.activation(gv[:, par, n0:n1], pg[half][:],
                                             AF.Silu)
                s_ac.inc(ins)
        if rms:
            yv = ybuf[:].rearrange("p (c t) -> p c t", c=NCH)
            for c in range(NCH):
                act.wait_ge(s_dv.h, M["res"][(NG - 1, c)])
                if c >= 2:
                    act.wait_ge(s_pe.h, M["dmm"][(NG - 1, NCH - 1)] + c - 1)
                s_ac.inc(act.activation(sqv[:, c % 2, :], yv[:, c, :],
                                        AF.Square))
            act.wait_ge(s_dv.h, M["yvar"])
            act.activation(tmpf[:], tmpf[:], AF.Ln, bias=eps2_col[:])
            s_ac.inc(act.activation(rstd[:], tmpf[:], AF.Exp, scale=-0.5))

    @block.tensor
    def _(pe):
        xnv = xnb[:].rearrange("p (c t) -> p c t", c=NCH)
        sqv = sq[:].rearrange("p (b t) -> p b t", b=2)
        pe.wait_ge(s_dv.h, M["xb"])
        for c in range(NCH):
            pe.wait_ge(s_ac.h, c + 1)
            st, sp = (c == 0), (c == NCH - 1)
            for half in range(2):
                n0, n1 = NHALF[half]
                pe.matmul(pp[half][:], ones_bf[:], xnv[:, c, n0:n1],
                          start=st, stop=sp)
                ins = pe.matmul(pp[2 + half][:], ones_bf[:],
                                sqv[:, c % 2, n0:n1], start=st, stop=sp)
            s_pe.inc(ins)
        hv = hbuf[:].rearrange("p (k t) -> p k t", k=GCH)
        yv = (ybuf[:].rearrange("p (c t) -> p c t", c=NCH) if rms else None)
        for g in range(NG):
            for hc in range(GCH):
                ch = GCH * g + hc
                par = ch % 2
                pg = pp[0 + 4 * par:2 + 4 * par]
                pu = pp[2 + 4 * par:4 + 4 * par]
                if ch == 0:
                    pe.wait_ge(s_dv.h, M["xn"])
                pe.wait_ge(s_wg.h, wg_cnt[("g", ch)])
                pe.wait_ge(s_wu.h, wu_cnt[ch])
                if ch >= 2:
                    pe.wait_ge(s_ac.h, M["silu"][ch - 2])
                    pe.wait_ge(s_dv.h, M["h"][ch - 2])
                if g > 0 and hc < 2:
                    pe.wait_ge(s_dv.h, M["res"][(g - 1, NCH - 1)])
                wgv = wg_buf[:].rearrange("p (b c o) -> p b c o", b=2, c=NCH)
                wuv = wu_buf[:].rearrange("p (b c o) -> p b c o", b=2, c=NCH)
                last = None
                for half in range(2):
                    n0, n1 = NHALF[half]
                    for c in range(NCH):
                        last = pe.matmul(pg[half][:], wgv[:, par, c, :],
                                         xnv[:, c, n0:n1],
                                         start=(c == 0), stop=(c == NCH - 1))
                s_pe.inc(last)
                for half in range(2):
                    n0, n1 = NHALF[half]
                    for c in range(NCH):
                        last = pe.matmul(pu[half][:], wuv[:, par, c, :],
                                         xnv[:, c, n0:n1],
                                         start=(c == 0), stop=(c == NCH - 1))
                s_pe.inc(last)
            for m in range(NCH):
                par = m % 2
                pd = pp[2 * par:2 * par + 2]
                pe.wait_ge(s_wg.h, wg_cnt[("d", g, m)])
                pe.wait_ge(s_dv.h, M["h"][GCH * (g + 1) - 1])
                if m >= 2:
                    pe.wait_ge(s_dv.h, M["res"][(g, m - 2)])
                elif g > 0 and m < 2:
                    pe.wait_ge(s_dv.h, M["res"][(g - 1, NCH - 1)])
                pe.wait_ge(s_ac.h, M["silu"][GCH * (g + 1) - 1])
                wdv = wd_buf[:].rearrange("p (b k o) -> p b k o", b=2, k=GCH)
                last = None
                for half in range(2):
                    n0, n1 = NHALF[half]
                    for k in range(GCH):
                        last = pe.matmul(pd[half][:], wdv[:, par, k, :],
                                         hv[:, k, n0:n1],
                                         start=(k == 0), stop=(k == GCH - 1))
                s_pe.inc(last)
        if rms:
            for c in range(NCH):
                pe.wait_ge(s_ac.h, M["ysq0"] + c + 1)
                st, sp = (c == 0), (c == NCH - 1)
                for half in range(2):
                    n0, n1 = NHALF[half]
                    ins = pe.matmul(pp[4 + half][:], ones_bf[:],
                                    sqv[:, c % 2, n0:n1], start=st, stop=sp)
                s_pe.inc(ins)

    @block.vector
    def _(v):
        xv = xT[:].rearrange("p (c t) -> p c t", c=NCH)
        xnv = xnb[:].rearrange("p (c t) -> p c t", c=NCH)
        v.memset(ones_bf[:], 1.0)
        v.memset(eps_col[:], EPS_LN)
        if rms:
            v.memset(eps2_col[:], EPS_RMS)
        if layer == "l0":
            pass
        v.wait_ge(s_x.h, s_x.n)
        if layer == "l0":
            v.tensor_scalar(sd05[:], sd_cols[:], 0.5, None, ALU.mult)
        for c in range(NCH):
            ins = v.tensor_copy(xnv[:, c, :], xv[:, c, :])
        s_dv.inc(ins)
        v.wait_ge(s_pe.h, M["stats"])
        for half in range(2):
            n0, n1 = NHALF[half]
            v.tensor_scalar(m_sb[:, n0:n1], pp[half][:], 1.0 / D, None,
                            ALU.mult)
        v.tensor_mul(tmpf[:], m_sb[:], m_sb[:])
        for half in range(2):
            n0, n1 = NHALF[half]
            ins = v.scalar_tensor_tensor(tmpf[:, n0:n1], pp[2 + half][:],
                                         1.0 / D, tmpf[:, n0:n1],
                                         ALU.mult, ALU.subtract)
        s_dv.inc(ins)
        v.wait_ge(s_ac.h, M["rstd"])
        v.tensor_mul(m_sb[:], m_sb[:], rstd[:])
        for c in range(NCH):
            v.tensor_mul(tmpf[:], xv[:, c, :], rstd[:])
            ins = v.tensor_sub(xnv[:, c, :], tmpf[:], m_sb[:])
        s_dv.inc(ins)
        hv = hbuf[:].rearrange("p (k t) -> p k t", k=GCH)
        gv = gsil[:].rearrange("p (b t) -> p b t", b=2)
        yv = (ybuf[:].rearrange("p (c t) -> p c t", c=NCH) if rms else None)
        for g in range(NG):
            for hc in range(GCH):
                ch = GCH * g + hc
                par = ch % 2
                pu = pp[2 + 4 * par:4 + 4 * par]
                v.wait_ge(s_ac.h, M["silu"][ch])
                v.wait_ge(s_pe.h, M["umm"][ch])
                for half in range(2):
                    n0, n1 = NHALF[half]
                    if layer == "l0":
                        ins = v.scalar_tensor_tensor(
                            hv[:, hc, n0:n1], pu[half][:],
                            su_cols[:, ch:ch + 1], gv[:, par, n0:n1],
                            ALU.mult, ALU.mult)
                    else:
                        ins = v.tensor_tensor(hv[:, hc, n0:n1],
                                              gv[:, par, n0:n1], pu[half][:],
                                              ALU.mult)
                s_dv.inc(ins)
            for m in range(NCH):
                par = m % 2
                pd = pp[2 * par:2 * par + 2]
                v.wait_ge(s_pe.h, M["dmm"][(g, m)])
                for half in range(2):
                    n0, n1 = NHALF[half]
                    if rms:
                        if g == 0:
                            ins = v.tensor_copy(yv[:, m, n0:n1], pd[half][:])
                        else:
                            ins = v.tensor_add(yv[:, m, n0:n1],
                                               yv[:, m, n0:n1], pd[half][:])
                    else:
                        ins = v.scalar_tensor_tensor(
                            xv[:, m, n0:n1], pd[half][:],
                            sd05[:, m:m + 1], xv[:, m, n0:n1],
                            ALU.mult, ALU.add)
                s_dv.inc(ins)
        if rms:
            v.wait_ge(s_pe.h, M["ystats"])
            for half in range(2):
                n0, n1 = NHALF[half]
                ins = v.tensor_scalar(tmpf[:, n0:n1], pp[4 + half][:], 1.0 / D,
                                      None, ALU.mult)
            s_dv.inc(ins)
            v.wait_ge(s_ac.h, M["yrstd"])
            v.tensor_scalar(rstd05[:], rstd[:], 0.5, None, ALU.mult)
            for c in range(NCH):
                v.tensor_mul(tmpf[:], yv[:, c, :], rstd05[:])
                ins = v.tensor_add(xv[:, c, :], xv[:, c, :], tmpf[:])
            s_dv.inc(ins)

    return M



def emit_attn0(nc, S, aps):
    sb = lambda n, shp, dt=F32: S.enter_context(nc.sbuf_tensor(n, shp, dt))
    ps = lambda n, shp, dt=F32: S.enter_context(nc.psum_tensor(n, shp, dt))
    sem = lambda n: Sem(S.enter_context(nc.semaphore(n)))

    NQ, NK = 16, 8
    INV2PI = 1.0 / (2.0 * PI)
    TCG = [(0, 1, 2), (3, 4, 5), (6, 7)]
    NHALF = [(0, 512), (512, 1024)]

    xT = sb("xT", [128, NCH * T])
    xnb = sb("xnb", [128, NCH * T], BF16)
    sq = sb("sq", [128, 2 * T], BF16)
    m_sb = sb("m_sb", [128, T])
    rstd = sb("rstd", [128, T])
    tmpf = sb("tmpf", [128, T])
    ones_bf = sb("ones_bf", [128, 128], BF16)
    onesc_bf = sb("onesc_bf", [1, 128], BF16)
    eps_col = sb("eps_col", [128, 1])
    qT = sb("qT", [128, NQ * T], BF16)
    kT = sb("kT", [128, NK * T], BF16)
    vtm = sb("vtm", [128, 8 * T], BF16)
    wbuf = sb("wbuf", [128, 2 * NCH * 128], BF16)
    vbuf = sb("vbuf", [128, 2 * T], BF16)
    cos_bf = sb("cos_bf", [128, T], BF16)
    sin_bf = sb("sin_bf", [128, T], BF16)
    rtt_bf = sb("rtt_bf", [128, 128], BF16)
    prot_bf = sb("prot_bf", [128, 128], BF16)
    t32_bf = sb("t32_bf", [128, 33 * 31 + 32], BF16)
    ident_bf = sb("ident_bf", [128, 128], BF16)

    bq_c = sb("bq_c", [128, 16])
    bk_c = sb("bk_c", [128, 8])
    bo_c = sb("bo_c", [128, 16])
    bv_bf = sb("bv_bf", [1, T], BF16)
    nsq_q = sb("nsq_q", [16, T], BF16)
    nsq_k = sb("nsq_k", [8, T], BF16)
    prodb = sb("prodb", [128, 2 * T], BF16)
    asb = None
    invrow = None
    qsc = prodb[:, 0:T]
    asb = vbuf[0:32, 0:T]
    invrow = vbuf[0:1, T:2 * T]
    bc_sb = sq[:, 0:T]
    p_sb = sq[:, T:2 * T]
    a_tm = m_sb[:]
    att_tm = rstd[:].bitcast(BF16)

    pp = [ps(f"pp{i}", [128, 512]) for i in range(6)]
    psTA = ps("psTA", [128, 32], BF16)
    psTT = ps("psTT", [128, 128], BF16)

    s_x = sem("s_x")
    s_w = sem("s_w")
    s_cst = sem("s_cst")
    s_pe = sem("s_pe")
    s_ac = sem("s_ac")
    s_dv = sem("s_dv")
    s_out = sem("s_out")

    M = {"xb": 1, "var": 2, "xn": 3, "stats": 16, "rstd": 17}
    dv, ac, pe = 3, 17, 16
    M["qmm"] = {}; M["qev"] = {}; M["qrot"] = {}; M["rope_q"] = {}
    for oc in range(NQ):
        pe += 1; M["qmm"][oc] = pe
        dv += 1; M["qev"][oc] = dv
        pe += 1; M["qrot"][oc] = pe
        dv += 1; M["rope_q"][oc] = dv
    M["kmm"] = {}; M["kev"] = {}; M["krot"] = {}; M["rope_k"] = {}
    for j in range(NK):
        pe += 1; M["kmm"][j] = pe
        dv += 1; M["kev"][j] = dv
        pe += 1; M["krot"][j] = pe
        dv += 1; M["rope_k"][j] = dv
    M["vmm"] = {}; M["vdone"] = {}; M["vev"] = {}
    for p in range(3):
        for c in range(NCH):
            pe += 1
            M["vmm"][(p, c)] = pe
        pe += 1
        M["vdone"][p] = pe
        ac += 1; M["vev"][p] = ac
    M["n2sq"] = {}; M["n2mm"] = {}
    for hh in range(NQ + NK):
        ac += 1; M["n2sq"][hh] = ac
        pe += 1; M["n2mm"][hh] = pe
    dv += 1; M["inv_ev"] = dv
    ac += 1; M["inv_act"] = ac
    M["pmm"] = {}; M["fr"] = {}; M["fa"] = {}; M["fd"] = {}; M["fb"] = {}
    M["bcev"] = {}
    for hh in range(NQ + NK):
        pe += 1; M["pmm"][hh] = pe
        ac += 1; M["bcev"][hh] = ac
        dv += 1; M["fr"][hh] = dv
        ac += 1; M["fa"][hh] = ac
        dv += 1; M["fd"][hh] = dv
        ac += 1; M["fb"][hh] = ac
    M["prod"] = {}; M["prodmm"] = {}
    M["aevac"] = {}; M["atr"] = {}; M["atev"] = {}
    for g in range(4):
        for pl in range(32):
            dv += 1; M["prod"][(g, pl)] = dv
            pe += 1; M["prodmm"][(g, pl)] = pe
        ac += 1; M["aevac"][g] = ac
        for tt in range(8):
            pe += 1; M["atr"][(g, tt)] = pe
            ac += 1; M["atev"][(g, tt)] = ac
    M["s2"] = {}; M["s2t"] = {}; M["attev"] = {}
    for tt in range(8):
        for h in range(NQ):
            dv += 1; M["s2"][(tt, h)] = dv
            pe += 1; M["s2t"][(tt, h)] = pe
            ac += 1; M["attev"][(tt, h)] = ac

    wsched = ([("q", oc) for oc in range(NQ)] + [("k", j) for j in range(NK)]
              + [("v", p, c) for p in range(3) for c in range(NCH)]
              + [("ir", hh) for hh in range(NQ + NK)])
    wcnt = {}
    n_w = 0
    for s_ in wsched:
        n_w += 16
        wcnt[s_] = n_w

    block = S.enter_context(nc.Block())
    xv_ = lambda t_: t_[:].rearrange("p (c t) -> p c t", c=NCH)

    @block.sync
    def _(sync):
        s_x.inc(sync.dma_start(
            out=xT[:].rearrange("p (c t) -> p c t", c=NCH),
            in_=aps["x"].rearrange("(c p) t -> p c t", p=128)), 16)
        for dst, src in [(bq_c, aps["bq"]), (bk_c, aps["bk"])]:
            s_x.inc(sync.dma_start(out=dst[:], in_=src), 16)
        sync.wait_ge(s_ac.h, M["attev"][(7, NQ - 1)])
        s_out.inc(sync.dma_start(
            out=aps["att_out"].rearrange("(c p) t -> p c t", p=128),
            in_=xnb[:].rearrange("p (c t) -> p c t", c=NCH)), 16)
        sync.wait_ge(s_out.h, 16)

    @block.gpsimd
    def _(gp):
        for dst, src in [(cos_bf[:], aps["cos"]), (sin_bf[:], aps["sin"]),
                         (rtt_bf[:], aps["rtt"]), (t32_bf[:], aps["t32"]),
                         (ident_bf[:], aps["ident"]), (bv_bf[:], aps["bv"]),
                         (prot_bf[:], aps["prot"])]:
            s_cst.inc(gp.dma_start(out=dst, in_=src), 16)
        wv = wbuf[:].rearrange("p (b c o) -> p b c o", b=2, c=NCH)
        vbv = vbuf[:].rearrange("p (b t) -> p b t", b=2)
        qko_idx = 0
        for s_ in wsched:
            kind = s_[0]
            if kind == "v":
                _, p, c = s_
                vidx = p * NCH + c
                if vidx >= 2:
                    pc2 = (p, c - 2) if c >= 2 else (p - 1, NCH + c - 2)
                    gp.wait_ge(s_pe.h, M["vmm"][pc2])
                src = aps["WvT"][128 * c:128 * (c + 1), :]
                s_w.inc(gp.dma_start(out=vbv[:, vidx % 2, :], in_=src), 16)
                continue
            if kind == "ir":
                hh = s_[1]
                if hh == 0:
                    gp.wait_ge(s_ac.h, M["inv_act"])
                else:
                    gp.wait_ge(s_ac.h, M["bcev"][hh - 1])
                src = nsq_q[hh:hh + 1, :] if hh < NQ else \
                    nsq_k[hh - NQ:hh - NQ + 1, :]
                s_w.inc(gp.dma_start(out=invrow[:], in_=src), 16)
                continue
            par = qko_idx % 2
            if kind == "q":
                oc = s_[1]
                if qko_idx >= 2:
                    gp.wait_ge(s_pe.h, M["qmm"][oc - 2])
                src = aps["WqT"][:, 128 * oc:128 * (oc + 1)]
            elif kind == "k":
                j = s_[1]
                prev = qko_idx - 2
                gp.wait_ge(s_pe.h, M["kmm"][prev - NQ] if prev >= NQ
                           else M["qmm"][prev])
                src = aps["WkT"][:, 128 * j:128 * (j + 1)]
            srcv = src.rearrange("(c p) o -> p c o", p=128)
            s_w.inc(gp.dma_start(out=wv[:, par, :, :], in_=srcv), 16)
            qko_idx += 1

    @block.scalar
    def _(act):
        xnv = xv_(xnb)
        sqv = sq[:].rearrange("p (b t) -> p b t", b=2)
        act.wait_ge(s_dv.h, M["xb"])
        for c in range(NCH):
            if c >= 2:
                act.wait_ge(s_pe.h, c - 1)
            s_ac.inc(act.activation(sqv[:, c % 2, :], xnv[:, c, :], AF.Square))
        act.wait_ge(s_dv.h, M["var"])
        act.activation(tmpf[:], tmpf[:], AF.Ln, bias=eps_col[:])
        s_ac.inc(act.activation(rstd[:], tmpf[:], AF.Exp, scale=-0.5))
        for p in range(3):
            act.wait_ge(s_pe.h, M["vdone"][p])
            for tci, tc in enumerate(TCG[p]):
                for half in range(2):
                    n0, n1 = NHALF[half]
                    ins = act.activation(
                        vtm[:, T * tc + n0:T * tc + n1],
                        pp[2 * tci + half][:], AF.Copy)
            s_ac.inc(ins)
        for hh in range(NQ + NK):
            srct = qT[:, T * hh:T * (hh + 1)] if hh < NQ else \
                kT[:, T * (hh - NQ):T * (hh - NQ + 1)]
            if hh == 0:
                act.wait_ge(s_dv.h, M["rope_k"][NK - 1])
            if hh >= 2:
                act.wait_ge(s_pe.h, M["n2mm"][hh - 2])
            s_ac.inc(act.activation(sqv[:, hh % 2, :], srct, AF.Square))
        act.wait_ge(s_dv.h, M["inv_ev"])
        act.activation(tmpf[0:16, :], nsq_q[:], AF.Ln)
        act.activation(nsq_q[:], tmpf[0:16, :], AF.Exp, scale=-0.5)
        act.activation(tmpf[0:16, :], nsq_k[:], AF.Ln) if False else None
        act.activation(tmpf[0:8, :], nsq_k[:], AF.Ln)
        s_ac.inc(act.activation(nsq_k[:], tmpf[0:8, :], AF.Exp, scale=-0.5))
        for hh in range(NQ + NK):
            ft = qT[:, T * hh:T * (hh + 1)] if hh < NQ else \
                kT[:, T * (hh - NQ):T * (hh - NQ + 1)]
            act.wait_ge(s_pe.h, M["pmm"][hh])
            for half in range(2):
                n0, n1 = NHALF[half]
                ins = act.activation(bc_sb[:, n0:n1], pp[4 + half][:], AF.Copy)
            s_ac.inc(ins)
            act.wait_ge(s_dv.h, M["fr"][hh])
            act.activation(ft[0:64, :], rstd[0:64, :], AF.Sin)
            s_ac.inc(act.activation(tmpf[64:128, :], rstd[64:128, :], AF.Abs))
            act.wait_ge(s_dv.h, M["fd"][hh])
            s_ac.inc(act.activation(ft[64:128, :], tmpf[64:128, :], AF.Sin))
        for g in range(4):
            act.wait_ge(s_pe.h, M["prodmm"][(g, 31)])
            if g > 0:
                act.wait_ge(s_pe.h, M["atr"][(g - 1, 7)])
            for half in range(2):
                n0, n1 = NHALF[half]
                ins = act.activation(asb[:, n0:n1], pp[half][0:32, :],
                                     AF.Copy, scale=1.0 / 64.0)
            s_ac.inc(ins)
            for tt in range(8):
                act.wait_ge(s_pe.h, M["atr"][(g, tt)])
                s_ac.inc(act.activation(
                    a_tm[:, 128 * tt + 32 * g:128 * tt + 32 * (g + 1)],
                    psTA[:, 0:32], AF.Copy))
        for tt in range(8):
            for h in range(NQ):
                act.wait_ge(s_pe.h, M["s2t"][(tt, h)])
                s_ac.inc(act.activation(
                    xnv[:, h, 128 * tt:128 * (tt + 1)], psTT[:], AF.Copy))

    @block.tensor
    def _(pe_):
        xnv = xv_(xnb)
        sqv = sq[:].rearrange("p (b t) -> p b t", b=2)
        wv = wbuf[:].rearrange("p (b c o) -> p b c o", b=2, c=NCH)
        vbv = vbuf[:].rearrange("p (b t) -> p b t", b=2)
        pe_.wait_ge(s_dv.h, M["xb"])
        pe_.wait_ge(s_cst.h, 16 * 7)
        for c in range(NCH):
            pe_.wait_ge(s_ac.h, c + 1)
            st, sp = (c == 0), (c == NCH - 1)
            for half in range(2):
                n0, n1 = NHALF[half]
                pe_.matmul(pp[half][:], ones_bf[:], xnv[:, c, n0:n1],
                           start=st, stop=sp)
                ins = pe_.matmul(pp[2 + half][:], ones_bf[:],
                                 sqv[:, c % 2, n0:n1], start=st, stop=sp)
            s_pe.inc(ins)
        qko_idx = 0
        for kind, nch_out in [("q", NQ), ("k", NK)]:
            ev_ms = M["qev"] if kind == "q" else M["kev"]
            for oc in range(nch_out):
                par = qko_idx % 2
                pqp = pp[2 * par:2 * par + 2]
                pe_.wait_ge(s_w.h, wcnt[(kind, oc)])
                if qko_idx <= 1:
                    pe_.wait_ge(s_dv.h, M["xn"])
                if qko_idx >= 2:
                    prev = qko_idx - 2
                    pe_.wait_ge(s_dv.h, M["rope_q"][prev] if prev < NQ
                                else M["rope_k"][prev - NQ])
                last = None
                for half in range(2):
                    n0, n1 = NHALF[half]
                    for c in range(NCH):
                        last = pe_.matmul(pqp[half][:], wv[:, par, c, :],
                                          xnv[:, c, n0:n1],
                                          start=(c == 0), stop=(c == NCH - 1))
                s_pe.inc(last)
                pe_.wait_ge(s_dv.h, ev_ms[oc])
                last = None
                for half in range(2):
                    n0, n1 = NHALF[half]
                    last = pe_.matmul(pqp[half][:], prot_bf[:], qsc[:, n0:n1],
                                      start=True, stop=True)
                s_pe.inc(last)
                qko_idx += 1
        for p in range(3):
            if p > 0:
                pe_.wait_ge(s_ac.h, M["vev"][p - 1])
            else:
                pe_.wait_ge(s_dv.h, M["rope_k"][NK - 1])
            for c in range(NCH):
                vidx = p * NCH + c
                pe_.wait_ge(s_w.h, wcnt[("v", p, c)])
                last = None
                for tci, tc in enumerate(TCG[p]):
                    for half in range(2):
                        n0, n1 = NHALF[half]
                        last = pe_.matmul(
                            pp[2 * tci + half][:],
                            xnv[:, c, 128 * tc:128 * (tc + 1)],
                            vbv[:, vidx % 2, n0:n1],
                            start=(c == 0), stop=False)
                s_pe.inc(last)
            last = None
            for tci, tc in enumerate(TCG[p]):
                for half in range(2):
                    n0, n1 = NHALF[half]
                    last = pe_.matmul(pp[2 * tci + half][:], onesc_bf[:],
                                      bv_bf[0:1, n0:n1],
                                      start=False, stop=True)
            s_pe.inc(last)
        pe_.wait_ge(s_ac.h, M["vev"][2])
        for hh in range(NQ + NK):
            pe_.wait_ge(s_ac.h, M["n2sq"][hh])
            isq = hh < NQ
            row = hh if isq else hh - NQ
            base = 0 if isq else 2
            st = (hh == 0) or (hh == NQ)
            sp = (hh == NQ - 1) or (hh == NQ + NK - 1)
            last = None
            for half in range(2):
                n0, n1 = NHALF[half]
                last = pe_.matmul(pp[base + half][0:32, :],
                                  t32_bf[:, 32 * row:32 * row + 32],
                                  sqv[:, hh % 2, n0:n1], start=st, stop=sp)
            s_pe.inc(last)
        for hh in range(NQ + NK):
            par = hh % 2
            pqp = pp[2 * par:2 * par + 2]
            if hh < 2:
                pe_.wait_ge(s_dv.h, M["inv_ev"])
                pe_.wait_ge(s_ac.h, M["inv_act"])
            else:
                pe_.wait_ge(s_dv.h, M["fr"][hh - 2])
            if hh > 0:
                pe_.wait_ge(s_ac.h, M["bcev"][hh - 1])
            pe_.wait_ge(s_w.h, wcnt[("ir", hh)])
            srct = qT[:, T * hh:T * (hh + 1)] if hh < NQ else \
                kT[:, T * (hh - NQ):T * (hh - NQ + 1)]
            last = None
            for half in range(2):
                n0, n1 = NHALF[half]
                pe_.matmul(pp[4 + half][:], onesc_bf[:], invrow[0:1, n0:n1],
                           start=True, stop=True)
                last = pe_.matmul(pqp[half][:], rtt_bf[:], srct[:, n0:n1],
                                  start=True, stop=True)
            s_pe.inc(last)
        for g in range(4):
            for pl in range(32):
                pair = 32 * g + pl
                pe_.wait_ge(s_dv.h, M["prod"][(g, pl)])
                if pl == 0:
                    if g == 0:
                        pe_.wait_ge(s_dv.h, M["fr"][NQ + NK - 2])
                    else:
                        pe_.wait_ge(s_ac.h, M["aevac"][g - 1])
                last = None
                for half in range(2):
                    n0, n1 = NHALF[half]
                    last = pe_.matmul(pp[half][0:32, :],
                                      t32_bf[:, 32 * pl:32 * pl + 32],
                                      prodb[:, T * (pl % 2) + n0:
                                            T * (pl % 2) + n1],
                                      start=(pl == 0), stop=(pl == 31))
                s_pe.inc(last)
            for tt in range(8):
                pe_.wait_ge(s_ac.h, M["aevac"][g])
                if (g, tt) != (0, 0):
                    prev = (g, tt - 1) if tt > 0 else (g - 1, 7)
                    pe_.wait_ge(s_ac.h, M["atev"][prev])
                s_pe.inc(pe_.transpose(
                    psTA[:, 0:32], asb[:, 128 * tt:128 * (tt + 1)],
                    ident_bf[0:32, 0:32]))
        for tt in range(8):
            for h in range(NQ):
                k_ = tt * NQ + h
                pe_.wait_ge(s_dv.h, M["s2"][(tt, h)])
                if k_ >= 1:
                    prev = (tt, h - 1) if h > 0 else (tt - 1, NQ - 1)
                    pe_.wait_ge(s_ac.h, M["attev"][prev])
                s_pe.inc(pe_.transpose(
                    psTT[:], att_tm[:, 128 * (k_ % 2):128 * (k_ % 2 + 1)],
                    ident_bf[:]))

    @block.vector
    def _(v):
        xv = xv_(xT)
        xnv = xv_(xnb)
        v.memset(ones_bf[:], 1.0)
        v.memset(onesc_bf[:], 1.0)
        v.memset(eps_col[:], EPS_LN)
        v.wait_ge(s_x.h, s_x.n)
        for c in range(NCH):
            ins = v.tensor_copy(xnv[:, c, :], xv[:, c, :])
        s_dv.inc(ins)
        v.wait_ge(s_pe.h, M["stats"])
        for half in range(2):
            n0, n1 = NHALF[half]
            v.tensor_scalar(m_sb[:, n0:n1], pp[half][:], 1.0 / D, None,
                            ALU.mult)
        v.tensor_mul(tmpf[:], m_sb[:], m_sb[:])
        for half in range(2):
            n0, n1 = NHALF[half]
            ins = v.scalar_tensor_tensor(tmpf[:, n0:n1], pp[2 + half][:],
                                         1.0 / D, tmpf[:, n0:n1],
                                         ALU.mult, ALU.subtract)
        s_dv.inc(ins)
        v.wait_ge(s_ac.h, M["rstd"])
        v.tensor_mul(m_sb[:], m_sb[:], rstd[:])
        for c in range(NCH):
            v.tensor_mul(tmpf[:], xv[:, c, :], rstd[:])
            ins = v.tensor_sub(xnv[:, c, :], tmpf[:], m_sb[:])
        s_dv.inc(ins)
        qko_idx = 0
        for kind, nch_out, mm_ms, bias_c in [("q", NQ, M["qmm"], bq_c),
                                             ("k", NK, M["kmm"], bk_c)]:
            rot_ms = M["qrot"] if kind == "q" else M["krot"]
            tgt = qT if kind == "q" else kT
            for oc in range(nch_out):
                pqp = pp[0 + 2 * (qko_idx % 2):2 + 2 * (qko_idx % 2)]
                v.wait_ge(s_pe.h, mm_ms[oc])
                for half in range(2):
                    n0, n1 = NHALF[half]
                    ins = v.tensor_scalar(qsc[:, n0:n1], pqp[half][:],
                                          bias_c[:, oc:oc + 1], None, ALU.add)
                s_dv.inc(ins)
                qd = tgt[:, T * oc:T * (oc + 1)]
                v.tensor_mul(m_sb[:], qsc[:], cos_bf[:])
                v.wait_ge(s_pe.h, rot_ms[oc])
                for half in range(2):
                    n0, n1 = NHALF[half]
                    v.tensor_mul(rstd[:, n0:n1], pqp[half][:],
                                 sin_bf[:, n0:n1])
                ins = v.tensor_add(qd, m_sb[:], rstd[:])
                s_dv.inc(ins)
                qko_idx += 1
        v.wait_ge(s_pe.h, M["n2mm"][NQ + NK - 1])
        for half in range(2):
            n0, n1 = NHALF[half]
            v.tensor_scalar(nsq_q[:, n0:n1], pp[half][0:16, :], 1e-12, None,
                            ALU.max)
            ins = v.tensor_scalar(nsq_k[:, n0:n1], pp[2 + half][0:8, :],
                                  1e-12, None, ALU.max)
        s_dv.inc(ins)
        for hh in range(NQ + NK):
            par = hh % 2
            pqp = pp[2 * par:2 * par + 2]
            v.wait_ge(s_pe.h, M["pmm"][hh])
            v.wait_ge(s_ac.h, M["bcev"][hh])
            for half in range(2):
                n0, n1 = NHALF[half]
                v.tensor_mul(p_sb[:, n0:n1], pqp[half][:], bc_sb[:, n0:n1])
            v.tensor_scalar(m_sb[:], p_sb[:], INV2PI, None, ALU.mult)
            v.tensor_copy(m_sb[:].bitcast(I32), m_sb[:])
            v.tensor_copy(m_sb[:], m_sb[:].bitcast(I32))
            ins = v.scalar_tensor_tensor(rstd[:], m_sb[:],
                                         -2.0 * PI, p_sb[:],
                                         ALU.mult, ALU.add)
            s_dv.inc(ins)
            v.wait_ge(s_ac.h, M["fa"][hh])
            s_dv.inc(v.tensor_scalar(tmpf[64:128, :], tmpf[64:128, :],
                                     -1.0, PI / 2.0, ALU.mult, ALU.add))
        for g in range(4):
            for pl in range(32):
                pair = 32 * g + pl
                h, j = divmod(pair, 8)
                v.wait_ge(s_ac.h, max(M["fb"][h], M["fb"][NQ + j]))
                if pair >= 2:
                    p2 = pair - 2
                    v.wait_ge(s_pe.h, M["prodmm"][(p2 // 32, p2 % 32)])
                s_dv.inc(v.tensor_mul(
                    prodb[:, T * (pl % 2):T * (pl % 2 + 1)],
                    qT[:, T * h:T * (h + 1)], kT[:, T * j:T * (j + 1)]))

        for tt in range(8):
            for h in range(NQ):
                k_ = tt * NQ + h
                g = h // 4
                at = att_tm[:, 128 * (k_ % 2):128 * (k_ % 2 + 1)]
                if k_ == 0:
                    for gg in range(4):
                        v.wait_ge(s_ac.h, M["atev"][(gg, 7)])
                if k_ >= 2:
                    prev = (tt, h - 2) if h >= 2 else (tt - 1, NQ - 2 + h)
                    v.wait_ge(s_pe.h, M["s2t"][prev])
                ins = None
                for j in range(8):
                    pair = 8 * h + j
                    acol = a_tm[:, 128 * tt + pair:128 * tt + pair + 1]
                    vsl = vtm[:, T * tt + 128 * j:T * tt + 128 * (j + 1)]
                    if j == 0:
                        v.tensor_scalar(at, vsl, acol, None, ALU.mult)
                    else:
                        ins = v.scalar_tensor_tensor(at, vsl, acol, at,
                                                     ALU.mult, ALU.add)
                s_dv.inc(ins)

    return M


def build_attn0():
    nc = bass.Bass("TRN2", target_bir_lowering=False, debug=False)
    aps = {}
    aps["x"] = nc.dram_tensor("x_in", [D, T], F32, kind="ExternalInput").ap()
    aps["att_out"] = nc.dram_tensor("att_out", [D, T], BF16,
                                    kind="ExternalOutput").ap()
    for n_, shp in [("WqT", [D, D]), ("WkT", [D, 1024]), ("WvT", [D, 1024]),
                    ("bq", [128, 16]), ("bk", [128, 8]),
                    ("bv", [1, 1024]), ("rtt", [128, 128]),
                    ("cos", [128, T]), ("sin", [128, T]),
                    ("t32", [128, 33 * 31 + 32]), ("ident", [128, 128]),
                    ("prot", [128, 128])]:
        aps[n_] = nc.dram_tensor(n_, shp, F32, kind="ExternalInput").ap()
    with ExitStack() as S:
        emit_attn0(nc, S, aps)
    return nc


def build_attn0b():
    nc = bass.Bass("TRN2", target_bir_lowering=False, debug=False)
    x = nc.dram_tensor("x_in", [D, T], F32, kind="ExternalInput").ap()
    attp = nc.dram_tensor("attp", [D, T], BF16, kind="ExternalInput").ap()
    WoT = nc.dram_tensor("WoT", [D, D], F32, kind="ExternalInput").ap()
    bo = nc.dram_tensor("bo", [128, 16], F32, kind="ExternalInput").ap()
    x_out = nc.dram_tensor("x_out", [D, T], F32, kind="ExternalOutput").ap()
    with ExitStack() as S:
        sb = lambda n, shp, dt=F32: S.enter_context(nc.sbuf_tensor(n, shp, dt))
        ps = lambda n, shp: S.enter_context(nc.psum_tensor(n, shp, F32))
        sem = lambda n: Sem(S.enter_context(nc.semaphore(n)))
        xT = sb("xT", [128, NCH * T])
        ab = sb("ab", [128, NCH * T], BF16)
        bo_c = sb("bo_c", [128, 16])
        wbuf = sb("wbuf", [128, 2 * NCH * 128], BF16)
        pp = [ps(f"pp{i}", [128, 512]) for i in range(4)]
        s_x = sem("s_x"); s_w = sem("s_w"); s_pe = sem("s_pe")
        s_dv = sem("s_dv"); s_out = sem("s_out")
        NHALF = [(0, 512), (512, 1024)]
        block = S.enter_context(nc.Block())

        @block.sync
        def _(sync):
            s_x.inc(sync.dma_start(
                out=xT[:].rearrange("p (c t) -> p c t", c=NCH),
                in_=x.rearrange("(c p) t -> p c t", p=128)), 16)
            s_x.inc(sync.dma_start(
                out=ab[:].rearrange("p (c t) -> p c t", c=NCH),
                in_=attp.rearrange("(c p) t -> p c t", p=128)), 16)
            s_x.inc(sync.dma_start(out=bo_c[:], in_=bo), 16)
            sync.wait_ge(s_dv.h, NCH)
            s_out.inc(sync.dma_start(
                out=x_out.rearrange("(c p) t -> p c t", p=128),
                in_=xT[:].rearrange("p (c t) -> p c t", c=NCH)), 16)
            sync.wait_ge(s_out.h, 16)

        @block.gpsimd
        def _(gp):
            wv = wbuf[:].rearrange("p (b c o) -> p b c o", b=2, c=NCH)
            for oc in range(NCH):
                if oc >= 2:
                    gp.wait_ge(s_pe.h, oc - 1)
                srcv = WoT[:, 128 * oc:128 * (oc + 1)].rearrange(
                    "(c p) o -> p c o", p=128)
                s_w.inc(gp.dma_start(out=wv[:, oc % 2, :, :], in_=srcv), 16)

        @block.tensor
        def _(pe_):
            av = ab[:].rearrange("p (c t) -> p c t", c=NCH)
            wv = wbuf[:].rearrange("p (b c o) -> p b c o", b=2, c=NCH)
            pe_.wait_ge(s_x.h, 48)
            for oc in range(NCH):
                par = oc % 2
                pop = pp[2 * par:2 * par + 2]
                pe_.wait_ge(s_w.h, 16 * (oc + 1))
                if oc >= 2:
                    pe_.wait_ge(s_dv.h, oc - 1)
                last = None
                for half in range(2):
                    n0, n1 = NHALF[half]
                    for c in range(NCH):
                        last = pe_.matmul(pop[half][:], wv[:, par, c, :],
                                          av[:, c, n0:n1],
                                          start=(c == 0), stop=(c == NCH - 1))
                s_pe.inc(last)

        @block.vector
        def _(v):
            xv = xT[:].rearrange("p (c t) -> p c t", c=NCH)
            for oc in range(NCH):
                par = oc % 2
                pop = pp[2 * par:2 * par + 2]
                v.wait_ge(s_pe.h, oc + 1)
                for half in range(2):
                    n0, n1 = NHALF[half]
                    ins = v.scalar_tensor_tensor(xv[:, oc, n0:n1],
                                                 pop[half][:],
                                                 bo_c[:, oc:oc + 1],
                                                 xv[:, oc, n0:n1],
                                                 ALU.add, ALU.add)
                s_dv.inc(ins)
    return nc





PAIRS1 = [(i, j) for i in range(4) for j in range(i + 1)]


def emit_attn1(nc, S, aps):
    sb = lambda n, shp, dt=F32: S.enter_context(nc.sbuf_tensor(n, shp, dt))
    ps = lambda n, shp, dt=F32: S.enter_context(nc.psum_tensor(n, shp, dt))
    sem = lambda n: Sem(S.enter_context(nc.semaphore(n)))

    NQ, NKV = 16, 4
    SCALE = 1.0 / np.sqrt(128.0)
    TCG = [(0, 1, 2, 3, 4), (5, 6, 7)]
    NHALF = [(0, 512), (512, 1024)]
    NP = len(PAIRS1)

    xT = sb("xT", [128, NCH * T])
    xnb = sb("xnb", [128, NCH * T], BF16)
    sq = sb("sq", [128, 2 * T], BF16)
    m_sb = sb("m_sb", [128, T])
    rstd = sb("rstd", [128, T])
    tmpf = sb("tmpf", [128, T])
    ones_bf = sb("ones_bf", [128, 128], BF16)
    eps_col = sb("eps_col", [128, 1])
    qT = sb("qT", [128, NQ * T], BF16)
    kT = sb("kT", [128, NKV * T], BF16)
    vtm = sb("vtm", [128, 8 * 512], BF16)
    wbuf = sb("wbuf", [128, 2 * NCH * 128], BF16)
    vbuf = sb("vbuf", [128, 2 * 512], BF16)
    t32_bf = sb("t32_bf", [128, 33 * 31 + 32], BF16)
    ident_bf = sb("ident_bf", [128, 128], BF16)
    prodb = sb("prodb", [128, 2 * 256], BF16)
    s_sb = sb("s_sb", [32, 5 * 256], BF16)
    s_tm = sb("s_tm", [128, 2 * 160])
    p_tm = sb("p_tm", [128, 2 * 160])
    mxs = sb("mxs", [128, 64])
    sms = sb("sms", [128, 64])
    att_tm = rstd[:].bitcast(BF16)

    pp = [ps(f"pp{i}", [128, 512]) for i in range(5)]
    psS = ps("psS", [32, 256])
    psTA = ps("psTA", [128, 32], BF16)
    psTT = ps("psTT", [128, 128], BF16)

    s_x = sem("s_x")
    s_w = sem("s_w")
    s_cst = sem("s_cst")
    s_pe = sem("s_pe")
    s_ac = sem("s_ac")
    s_dv = sem("s_dv")
    s_bar = sem("s_bar")
    s_out = sem("s_out")

    M = {"xb": 1, "var": 2, "xn": 3, "stats": 16, "rstd": 17}
    dv, ac, pe = 3, 17, 16
    M["qmm"] = {}; M["qev"] = {}
    for oc in range(NQ + NKV):
        pe += 1; M["qmm"][oc] = pe
        ac += 1; M["qev"][oc] = ac
    M["vmm"] = {}; M["vdone"] = {}; M["vev"] = {}
    for p in range(2):
        for c in range(NCH):
            pe += 1; M["vmm"][(p, c)] = pe
        M["vdone"][p] = pe
        ac += 1; M["vev"][p] = ac
    M["prod"] = {}; M["prodmm"] = {}; M["sev"] = {}
    M["str"] = {}; M["stev"] = {}
    for g in range(5):
        for pl in range(32):
            dv += 1; M["prod"][(g, pl)] = dv
            pe += 1; M["prodmm"][(g, pl)] = pe
        ac += 1; M["sev"][g] = ac
        for tile in range(2):
            pe += 1; M["str"][(g, tile)] = pe
            ac += 1; M["stev"][(g, tile)] = ac
    dv += 1; M["smax1"] = dv
    ac += 1; M["sexp"] = ac
    dv += 1; M["softmax"] = dv
    M["s2"] = {}; M["s2t"] = {}; M["attev"] = {}
    for tile in range(8):
        for h in range(NQ):
            dv += 1; M["s2"][(tile, h)] = dv
            pe += 1; M["s2t"][(tile, h)] = pe
            ac += 1; M["attev"][(tile, h)] = ac
    M["omm"] = {}; M["res"] = {}
    for oc in range(NCH):
        pe += 1; M["omm"][oc] = pe
        dv += 1; M["res"][oc] = dv
    M["final"] = dv
    M["_sched"] = (pe, ac, dv)
    M["_sems"] = (s_pe, s_ac, s_dv)

    wsched = ([("q", oc) for oc in range(NQ)] + [("k", j) for j in range(NKV)]
              + [("v", p, c) for p in range(2) for c in range(NCH)]
              + [("o", oc) for oc in range(NCH)])
    wcnt = {}
    n_w = 0
    for s_ in wsched:
        n_w += 16
        wcnt[s_] = n_w

    block = S.enter_context(nc.Block())
    xv_ = lambda t_: t_[:].rearrange("p (c t) -> p c t", c=NCH)

    @block.sync
    def _(sync):
        s_x.inc(sync.dma_start(
            out=xT[:].rearrange("p (c t) -> p c t", c=NCH),
            in_=aps["x"].rearrange("(c p) t -> p c t", p=128)), 16)
        sync.wait_ge(s_dv.h, M["final"])
        s_out.inc(sync.dma_start(
            out=aps["x_out"].rearrange("(c p) t -> p c t", p=128),
            in_=xT[:].rearrange("p (c t) -> p c t", c=NCH)), 16)
        for dbg, buf in [("dbg_q", qT), ("dbg_k", kT), ("dbg_v", vtm),
                         ("dbg_s", s_tm), ("dbg_p", p_tm), ("dbg_att", xnb),
                         ("dbg_sms", sms), ("dbg_mxs", mxs)]:
            if dbg in aps:
                s_out.inc(sync.dma_start(out=aps[dbg], in_=buf[:]), 16)
        sync.wait_ge(s_out.h, s_out.n)

    @block.gpsimd
    def _(gp):
        for dst, src in [(t32_bf[:], aps["t32"]), (ident_bf[:], aps["ident"])]:
            s_cst.inc(gp.dma_start(out=dst, in_=src), 16)
        wv = wbuf[:].rearrange("p (b c o) -> p b c o", b=2, c=NCH)
        vbv = vbuf[:].rearrange("p (b t) -> p b t", b=2)
        qko_idx = 0
        for s_ in wsched:
            kind = s_[0]
            if kind == "v":
                _, p, c = s_
                vidx = p * NCH + c
                if vidx >= 2:
                    pc2 = (p, c - 2) if c >= 2 else (p - 1, NCH + c - 2)
                    gp.wait_ge(s_pe.h, M["vmm"][pc2])
                src = aps["WvT"][128 * c:128 * (c + 1), :]
                s_w.inc(gp.dma_start(out=vbv[:, vidx % 2, :], in_=src), 16)
                continue
            par = qko_idx % 2
            if kind == "q":
                oc = s_[1]
                if qko_idx >= 2:
                    gp.wait_ge(s_pe.h, M["qmm"][oc - 2])
                src = aps["WqT"][:, 128 * oc:128 * (oc + 1)]
            elif kind == "k":
                j = s_[1]
                gp.wait_ge(s_pe.h, M["qmm"][qko_idx - 2])
                src = aps["WkT"][:, 128 * j:128 * (j + 1)]
            else:
                oc = s_[1]
                gp.wait_ge(s_pe.h, M["omm"][oc - 2] if oc >= 2
                           else M["qmm"][NQ + NKV - 2 + oc])
                src = aps["WpT"][:, 128 * oc:128 * (oc + 1)]
            srcv = src.rearrange("(c p) o -> p c o", p=128)
            s_w.inc(gp.dma_start(out=wv[:, par, :, :], in_=srcv), 16)
            qko_idx += 1

    @block.scalar
    def _(act):
        xnv = xv_(xnb)
        sqv = sq[:].rearrange("p (b t) -> p b t", b=2)
        act.wait_ge(s_dv.h, M["xb"])
        for c in range(NCH):
            if c >= 2:
                act.wait_ge(s_pe.h, c - 1)
            s_ac.inc(act.activation(sqv[:, c % 2, :], xnv[:, c, :], AF.Square))
        act.wait_ge(s_dv.h, M["var"])
        act.activation(tmpf[:], tmpf[:], AF.Ln, bias=eps_col[:])
        s_ac.inc(act.activation(rstd[:], tmpf[:], AF.Exp, scale=-0.5))
        for oc in range(NQ + NKV):
            par = oc % 2
            pqp = pp[2 * par:2 * par + 2]
            act.wait_ge(s_pe.h, M["qmm"][oc])
            tgt = qT[:, T * oc:T * (oc + 1)] if oc < NQ else \
                kT[:, T * (oc - NQ):T * (oc - NQ + 1)]
            for half in range(2):
                n0, n1 = NHALF[half]
                ins = act.activation(tgt[:, n0:n1], pqp[half][:], AF.Copy)
            s_ac.inc(ins)
        for p in range(2):
            act.wait_ge(s_pe.h, M["vdone"][p])
            for tci, tc in enumerate(TCG[p]):
                ins = act.activation(vtm[:, 512 * tc:512 * (tc + 1)],
                                     pp[tci][:], AF.Copy)
            s_ac.inc(ins)
        for g in range(5):
            act.wait_ge(s_pe.h, M["prodmm"][(g, 31)])
            s_ac.inc(act.activation(s_sb[:, 256 * g:256 * (g + 1)], psS[:],
                                    AF.Copy))
            for tile in range(2):
                act.wait_ge(s_pe.h, M["str"][(g, tile)])
                s_ac.inc(act.activation(
                    s_tm[:, 160 * tile + 32 * g:160 * tile + 32 * (g + 1)],
                    psTA[:, 0:32], AF.Copy))
        act.wait_ge(s_dv.h, M["smax1"])
        for tile in range(2):
            ins = act.activation(s_tm[:, 160 * tile:160 * (tile + 1)],
                                 p_tm[:, 160 * tile:160 * (tile + 1)],
                                 AF.Exp, scale=SCALE)
        s_ac.inc(ins)
        for tile in range(8):
            for h in range(NQ):
                act.wait_ge(s_pe.h, M["s2t"][(tile, h)])
                s_ac.inc(act.activation(
                    xnv[:, h, 128 * tile:128 * (tile + 1)], psTT[:], AF.Copy))

    @block.tensor
    def _(pe_):
        xnv = xv_(xnb)
        sqv = sq[:].rearrange("p (b t) -> p b t", b=2)
        wv = wbuf[:].rearrange("p (b c o) -> p b c o", b=2, c=NCH)
        vbv = vbuf[:].rearrange("p (b t) -> p b t", b=2)
        pe_.wait_ge(s_dv.h, M["xb"])
        pe_.wait_ge(s_cst.h, 16 * 2)
        for c in range(NCH):
            pe_.wait_ge(s_ac.h, c + 1)
            st, sp = (c == 0), (c == NCH - 1)
            for half in range(2):
                n0, n1 = NHALF[half]
                pe_.matmul(pp[half][:], ones_bf[:], xnv[:, c, n0:n1],
                           start=st, stop=sp)
                ins = pe_.matmul(pp[2 + half][:], ones_bf[:],
                                 sqv[:, c % 2, n0:n1], start=st, stop=sp)
            s_pe.inc(ins)
        for oc in range(NQ + NKV):
            par = oc % 2
            pqp = pp[2 * par:2 * par + 2]
            pe_.wait_ge(s_w.h, wcnt[("q", oc)] if oc < NQ
                        else wcnt[("k", oc - NQ)])
            if oc <= 1:
                pe_.wait_ge(s_dv.h, M["xn"])
            if oc >= 2:
                pe_.wait_ge(s_ac.h, M["qev"][oc - 2])
            last = None
            for half in range(2):
                n0, n1 = NHALF[half]
                for c in range(NCH):
                    last = pe_.matmul(pqp[half][:], wv[:, par, c, :],
                                      xnv[:, c, n0:n1],
                                      start=(c == 0), stop=(c == NCH - 1))
            s_pe.inc(last)
        for p in range(2):
            if p > 0:
                pe_.wait_ge(s_ac.h, M["vev"][p - 1])
            else:
                pe_.wait_ge(s_ac.h, M["qev"][NQ + NKV - 1])
            for c in range(NCH):
                vidx = p * NCH + c
                pe_.wait_ge(s_w.h, wcnt[("v", p, c)])
                last = None
                for tci, tc in enumerate(TCG[p]):
                    last = pe_.matmul(pp[tci][:],
                                      xnv[:, c, 128 * tc:128 * (tc + 1)],
                                      vbv[:, vidx % 2, :],
                                      start=(c == 0), stop=(c == NCH - 1))
                s_pe.inc(last)
        for g in range(5):
            for pl in range(32):
                r = 32 * g + pl
                pe_.wait_ge(s_dv.h, M["prod"][(g, pl)])
                if pl == 0 and g > 0:
                    pe_.wait_ge(s_ac.h, M["sev"][g - 1])
                last = pe_.matmul(psS[:], t32_bf[:, 32 * pl:32 * pl + 32],
                                  prodb[:, 256 * (pl % 2):256 * (pl % 2 + 1)],
                                  start=(pl == 0), stop=(pl == 31))
                s_pe.inc(last)
            for tile in range(2):
                pe_.wait_ge(s_ac.h, M["sev"][g])
                if (g, tile) != (0, 0):
                    prev = (g, tile - 1) if tile > 0 else (g - 1, 1)
                    pe_.wait_ge(s_ac.h, M["stev"][prev])
                s_pe.inc(pe_.transpose(
                    psTA[:, 0:32],
                    s_sb[:, 256 * g + 128 * tile:256 * g + 128 * (tile + 1)],
                    ident_bf[0:32, 0:32]))
        for tile in range(8):
            for h in range(NQ):
                k_ = tile * NQ + h
                pe_.wait_ge(s_dv.h, M["s2"][(tile, h)])
                if k_ >= 1:
                    prev = (tile, h - 1) if h > 0 else (tile - 1, NQ - 1)
                    pe_.wait_ge(s_ac.h, M["attev"][prev])
                s_pe.inc(pe_.transpose(
                    psTT[:], att_tm[:, 128 * (k_ % 2):128 * (k_ % 2 + 1)],
                    ident_bf[:]))
        for oc in range(NCH):
            par = oc % 2
            pop = pp[2 * par:2 * par + 2]
            pe_.wait_ge(s_w.h, wcnt[("o", oc)])
            pe_.wait_ge(s_ac.h, M["attev"][(7, oc)])
            if oc >= 2:
                pe_.wait_ge(s_dv.h, M["res"][oc - 2])
            last = None
            for half in range(2):
                n0, n1 = NHALF[half]
                for c in range(NCH):
                    last = pe_.matmul(pop[half][:], wv[:, par, c, :],
                                      xnv[:, c, n0:n1],
                                      start=(c == 0), stop=(c == NCH - 1))
            s_pe.inc(last)

    @block.vector
    def _(v):
        xv = xv_(xT)
        xnv = xv_(xnb)
        v.memset(ones_bf[:], 1.0)
        v.memset(eps_col[:], EPS_LN)
        v.wait_ge(s_x.h, s_x.n)
        for c in range(NCH):
            ins = v.tensor_copy(xnv[:, c, :], xv[:, c, :])
        s_dv.inc(ins)
        v.wait_ge(s_pe.h, M["stats"])
        for half in range(2):
            n0, n1 = NHALF[half]
            v.tensor_scalar(m_sb[:, n0:n1], pp[half][:], 1.0 / D, None,
                            ALU.mult)
        v.tensor_mul(tmpf[:], m_sb[:], m_sb[:])
        for half in range(2):
            n0, n1 = NHALF[half]
            ins = v.scalar_tensor_tensor(tmpf[:, n0:n1], pp[2 + half][:],
                                         1.0 / D, tmpf[:, n0:n1],
                                         ALU.mult, ALU.subtract)
        s_dv.inc(ins)
        v.wait_ge(s_ac.h, M["rstd"])
        v.tensor_mul(m_sb[:], m_sb[:], rstd[:])
        for c in range(NCH):
            v.tensor_mul(tmpf[:], xv[:, c, :], rstd[:])
            ins = v.tensor_sub(xnv[:, c, :], tmpf[:], m_sb[:])
        s_dv.inc(ins)
        for g in range(5):
            for pl in range(32):
                r = 32 * g + pl
                pairidx, h = divmod(r, 16)
                i_, j_ = PAIRS1[pairidx]
                gkv = h // 4
                v.wait_ge(s_ac.h, M["qev"][h])
                v.wait_ge(s_ac.h, M["qev"][NQ + gkv])
                if r >= 2:
                    r2 = r - 2
                    v.wait_ge(s_pe.h, M["prodmm"][(r2 // 32, r2 % 32)])
                ins = v.tensor_mul(
                    prodb[:, 256 * (pl % 2):256 * (pl % 2 + 1)],
                    qT[:, T * h + 256 * i_:T * h + 256 * (i_ + 1)],
                    kT[:, T * gkv + 256 * j_:T * gkv + 256 * (j_ + 1)])
                s_dv.inc(ins)
        def bar(ins_):
            s_bar.inc(ins_)
            v.wait_ge(s_bar.h, s_bar.n)
        for tile in range(2):
            v.wait_ge(s_ac.h, M["stev"][(4, tile)])
            stv = s_tm[:, 160 * tile:160 * (tile + 1)]
            ptv = p_tm[:, 160 * tile:160 * (tile + 1)]
            for i_ in range(4):
                p0 = PAIRS1.index((i_, 0))
                dst = mxs[:, 16 * i_:16 * (i_ + 1)]
                bar(v.tensor_copy(dst, stv[:, 16 * p0:16 * (p0 + 1)]))
                for j_ in range(1, i_ + 1):
                    pj = PAIRS1.index((i_, j_))
                    bar(v.tensor_max(dst, dst, stv[:, 16 * pj:16 * (pj + 1)]))
            for pairidx, (i_, j_) in enumerate(PAIRS1):
                ins = v.tensor_sub(ptv[:, 16 * pairidx:16 * (pairidx + 1)],
                                   stv[:, 16 * pairidx:16 * (pairidx + 1)],
                                   mxs[:, 16 * i_:16 * (i_ + 1)])
            if tile == 1:
                s_dv.inc(ins)
        v.wait_ge(s_ac.h, M["sexp"])
        for tile in range(2):
            etv = s_tm[:, 160 * tile:160 * (tile + 1)]
            ptv = p_tm[:, 160 * tile:160 * (tile + 1)]
            for i_ in range(4):
                p0 = PAIRS1.index((i_, 0))
                dst = sms[:, 16 * i_:16 * (i_ + 1)]
                bar(v.tensor_copy(dst, etv[:, 16 * p0:16 * (p0 + 1)]))
                for j_ in range(1, i_ + 1):
                    pj = PAIRS1.index((i_, j_))
                    bar(v.tensor_add(dst, dst, etv[:, 16 * pj:16 * (pj + 1)]))
            bar(v.reciprocal(mxs[:], sms[:]))
            for pairidx, (i_, j_) in enumerate(PAIRS1):
                ins = v.tensor_mul(ptv[:, 16 * pairidx:16 * (pairidx + 1)],
                                   etv[:, 16 * pairidx:16 * (pairidx + 1)],
                                   mxs[:, 16 * i_:16 * (i_ + 1)])
        s_dv.inc(ins)
        for tile in range(8):
            i_ = tile // 2
            half = tile % 2
            stile = half
            for h in range(NQ):
                k_ = tile * NQ + h
                gkv = h // 4
                at = att_tm[:, 128 * (k_ % 2):128 * (k_ % 2 + 1)]
                if k_ >= 2:
                    prev = (tile, h - 2) if h >= 2 else (tile - 1, NQ - 2 + h)
                    v.wait_ge(s_pe.h, M["s2t"][prev])
                ins = None
                first = True
                for j_ in range(i_ + 1):
                    pairidx = PAIRS1.index((i_, j_))
                    pcol = p_tm[:, 160 * stile + 16 * pairidx + h:
                                160 * stile + 16 * pairidx + h + 1]
                    vtile = 2 * j_ + half
                    vsl = vtm[:, 512 * vtile + 128 * gkv:
                              512 * vtile + 128 * (gkv + 1)]
                    if first:
                        ins = v.tensor_scalar(at, vsl, pcol, None, ALU.mult)
                        first = False
                    else:
                        ins = v.scalar_tensor_tensor(at, vsl, pcol, at,
                                                     ALU.mult, ALU.add)
                s_dv.inc(ins)
        for oc in range(NCH):
            par = oc % 2
            pop = pp[2 * par:2 * par + 2]
            v.wait_ge(s_pe.h, M["omm"][oc])
            for half in range(2):
                n0, n1 = NHALF[half]
                ins = v.tensor_tensor(xv[:, oc, n0:n1], xv[:, oc, n0:n1],
                                      pop[half][:], ALU.add)
            s_dv.inc(ins)

    return M


def build_attn1(debug_counts=False, debug_taps=False):
    nc = bass.Bass("TRN2", target_bir_lowering=False, debug=False)
    aps = {}
    aps["x"] = nc.dram_tensor("x_in", [D, T], F32, kind="ExternalInput").ap()
    aps["x_out"] = nc.dram_tensor("x_out", [D, T], F32, kind="ExternalOutput").ap()
    for n_, shp in [("WqT", [D, D]), ("WkT", [D, 512]), ("WvT", [D, 512]),
                    ("WpT", [D, D]), ("t32", [128, 33 * 31 + 32]),
                    ("ident", [128, 128])]:
        aps[n_] = nc.dram_tensor(n_, shp, F32, kind="ExternalInput").ap()
    if debug_taps:
        for n_, shp, dt_ in [("dbg_q", [128, 16 * T], BF16),
                             ("dbg_k", [128, 4 * T], BF16),
                             ("dbg_v", [128, 8 * 512], BF16),
                             ("dbg_s", [128, 2 * 160], F32),
                             ("dbg_p", [128, 2 * 160], F32),
                             ("dbg_att", [128, NCH * T], BF16),
                             ("dbg_sms", [128, 64], F32),
                             ("dbg_mxs", [128, 64], F32)]:
            aps[n_] = nc.dram_tensor(n_, shp, dt_, kind="ExternalOutput").ap()
    with ExitStack() as S:
        M = emit_attn1(nc, S, aps)
        if debug_counts:
            pe_s, ac_s, dv_s = M["_sched"]
            s_pe, s_ac, s_dv = M["_sems"]
            print("sched pe/ac/dv:", pe_s, ac_s, dv_s,
                  " emitted:", s_pe.n, s_ac.n, s_dv.n)
    return nc


def build_ffn(layer):
    nc = bass.Bass("TRN2", target_bir_lowering=False, debug=False)
    nhid = HID0
    ngroups = 2 if layer == "l0" else 4
    aps = {}
    aps["x_in"] = nc.dram_tensor("x_in", [D, T], F32, kind="ExternalInput").ap()
    aps["x_out"] = nc.dram_tensor("x_out", [D, T], F32, kind="ExternalOutput").ap()
    aps["gT"] = nc.dram_tensor("gT", [D, nhid], BF16, kind="ExternalInput").ap()
    aps["uT"] = nc.dram_tensor("uT", [D, nhid], BF16, kind="ExternalInput").ap()
    aps["dT"] = nc.dram_tensor("dT", [nhid, D], BF16, kind="ExternalInput").ap()
    if layer == "l0":
        aps["sg"] = nc.dram_tensor("sg", [128, nhid // 128], F32,
                                   kind="ExternalInput").ap()
        aps["su"] = nc.dram_tensor("su", [128, nhid // 128], F32,
                                   kind="ExternalInput").ap()
        aps["sd"] = nc.dram_tensor("sd", [128, NCH], F32,
                                   kind="ExternalInput").ap()
    with ExitStack() as S:
        emit_ffn(nc, S, layer, aps, ngroups=ngroups, nhid=nhid)
    return nc


def kernel(**inputs):
    sh, per_core = host_prep(inputs)

    nc_dq = build_dequant()
    res = run_cores(nc_dq, dequant_in_maps(sh, per_core))
    G = host_gather_dequant(res, per_core)
    gT1 = np.ascontiguousarray(G["G_gu"][:, :I1])
    uT1 = np.ascontiguousarray(G["G_gu"][:, I1:])

    x = [pc["x"] for pc in per_core]

    def run_ffn(layer, x):
        nc = build_ffn(layer)
        if layer == "l0":
            maps = [{"x_in": x[c], "gT": G["G_l0g"], "uT": G["G_l0u"],
                     "dT": G["G_l0d"], "sg": sh["l0_sg"], "su": sh["l0_su"],
                     "sd": sh["l0_sd"]} for c in range(NCORE)]
        else:
            maps = [{"x_in": x[c], "gT": gT1, "uT": uT1, "dT": G["G_dn"]}
                    for c in range(NCORE)]
        res = run_cores(nc, maps)
        return [res[c]["x_out"] for c in range(NCORE)]

    def run_attn0(x):
        nc = build_attn0()
        maps = []
        for c in range(NCORE):
            pc = per_core[c]
            maps.append({"x_in": x[c], "WqT": sh["l0_WqT"],
                         "WkT": sh["l0_WkT"], "WvT": sh["l0_WvT"],
                         "bq": sh["l0_bq"], "bk": sh["l0_bk"],
                         "bv": sh["l0_bv"], "rtt": sh["l0_RTT"],
                         "cos": pc["rope_cos"], "sin": pc["rope_sin"],
                         "t32": sh["t32"], "ident": sh["ident"],
                         "prot": sh["l0_prot"]})
        res = run_cores(nc, maps)
        O = np.empty((4, 16, 2048, 128), dtype=BF16NP)
        for c in range(NCORE):
            A = res[c]["att_out"]
            O[:, :, SL * c:SL * (c + 1), :] = (
                A.reshape(16, 128, 4, SL).transpose(2, 0, 3, 1))
        att_full = O.reshape(4, 2048, D)
        nc2 = build_attn0b()
        maps2 = []
        for c in range(NCORE):
            attp = np.ascontiguousarray(
                att_full[:, SL * c:SL * (c + 1), :]
                .transpose(2, 0, 1).reshape(D, T))
            maps2.append({"x_in": x[c], "attp": attp, "WoT": sh["l0_WoT"],
                          "bo": sh["l0_bo"]})
        res2 = run_cores(nc2, maps2)
        return [res2[c]["x_out"] for c in range(NCORE)]

    def run_attn1(x):
        nc = build_attn1()
        maps = [{"x_in": x[c], "WqT": sh["l1_WqT"], "WkT": sh["l1_WkT"],
                 "WvT": sh["l1_WvT"], "WpT": sh["l1_WpT"], "t32": sh["t32"],
                 "ident": sh["ident"]} for c in range(NCORE)]
        res = run_cores(nc, maps)
        return [res[c]["x_out"] for c in range(NCORE)]

    x = run_ffn("l0", x)
    x = run_attn0(x)
    x = run_ffn("l0", x)
    x = run_ffn("l1", x)
    x = run_attn1(x)
    x = run_ffn("l1", x)

    out = np.empty((4, 2048, D), dtype=np.float32)
    for c in range(NCORE):
        out[:, SL * c:SL * (c + 1), :] = (
            x[c].reshape(D, 4, SL).transpose(1, 2, 0))
    return out
